# version 12
# baseline (speedup 1.0000x reference)
import numpy as np
from contextlib import ExitStack

import concourse.bass as bass
import concourse.mybir as mybir
from concourse.bass_utils import run_bass_kernel_spmd

F32 = mybir.dt.float32
BF16 = mybir.dt.bfloat16
U8 = mybir.dt.uint8
AF = mybir.ActivationFunctionType
ALU = mybir.AluOpType

L = 2048
D = 1024
NT = 16
NCH = 8
NQ4 = 4
KT = 16
MASKW = 352
MASKR = 112
EPS = 1e-5


class Plan:
    def __init__(self):
        self.steps = []
        self.cnt = {}

    def op(self, eng, fn, waits=(), inc=None, k=1):
        for (s, v) in waits:
            if v is not None:
                assert v <= self.cnt.get(s, 0), (s, v, self.cnt.get(s, 0))
        c = None
        if inc is not None:
            self.cnt[inc] = self.cnt.get(inc, 0) + k
            c = self.cnt[inc]
        self.steps.append((eng, tuple(waits), fn, inc, k))
        return c

    def n(self, sem):
        return self.cnt.get(sem, 0)

    def replay(self, eng_key, handle, sems):
        waited = {}
        for eng, waits, fn, inc, k in self.steps:
            if eng != eng_key:
                continue
            for (s, v) in waits:
                if v is None or v <= 0 or waited.get(s, -1) >= v:
                    continue
                handle.wait_ge(sems[s], v)
                waited[s] = v
            inst = fn(handle)
            if inc is not None:
                inst.then_inc(sems[inc], k)


def build_nc(n_phases=6, dbg=False):
    nc = bass.Bass("TRN2", target_bir_lowering=False, debug=False)

    x_d = nc.dram_tensor("x", [L, D], F32, kind="ExternalInput").ap()
    w_d = {}
    for n in ["wr", "wk", "wv", "wo"]:
        w_d[n] = nc.dram_tensor(n, [NQ4, 128, NCH * 256], F32, kind="ExternalInput").ap()
    for n in ["wq", "wka", "wva"]:
        w_d[n] = nc.dram_tensor(n, [8, 128, NCH * 128], F32, kind="ExternalInput").ap()
    bq_d = nc.dram_tensor("bq", [128, 8], F32, kind="ExternalInput").ap()
    bk_d = nc.dram_tensor("bk", [128, 8], F32, kind="ExternalInput").ap()
    bv_d = nc.dram_tensor("bv", [1, D], F32, kind="ExternalInput").ap()
    bo_d = nc.dram_tensor("bo", [1, D], F32, kind="ExternalInput").ap()
    mur_d = nc.dram_tensor("mur", [128, 8], F32, kind="ExternalInput").ap()
    muk_d = nc.dram_tensor("muk", [128, 8], F32, kind="ExternalInput").ap()
    lam_d = nc.dram_tensor("lam", [128, 1], F32, kind="ExternalInput").ap()
    strips_d = nc.dram_tensor("strips", [KT, 128, MASKW], F32, kind="ExternalInput").ap()
    ident_d = nc.dram_tensor("ident", [128, 128], F32, kind="ExternalInput").ap()
    onesf_d = nc.dram_tensor("onesf", [128, 128], F32, kind="ExternalInput").ap()
    out_d = nc.dram_tensor("out", [L, D], F32, kind="ExternalOutput").ap()

    res = nc.alloc_sbuf_tensor("res", [128, NT * 1024], F32)
    wbuf = nc.alloc_sbuf_tensor("wbuf", [128, 2 * 2048], BF16)
    regA = nc.alloc_sbuf_tensor("regA", [128, 57472], U8)
    regB = nc.alloc_sbuf_tensor("regB", [128, 32768], U8)
    regC = nc.alloc_sbuf_tensor("regC", [128, 32768], U8)
    bqp = nc.alloc_sbuf_tensor("bqp", [128, 8], F32)
    bkp = nc.alloc_sbuf_tensor("bkp", [128, 8], F32)
    mur = nc.alloc_sbuf_tensor("mur_sb", [128, 8], F32)
    muk = nc.alloc_sbuf_tensor("muk_sb", [128, 8], F32)
    murm = nc.alloc_sbuf_tensor("murm", [128, 8], F32)
    mukm = nc.alloc_sbuf_tensor("mukm", [128, 8], F32)
    sumc = nc.alloc_sbuf_tensor("sumc", [128, 64], F32)
    sqc = nc.alloc_sbuf_tensor("sqc", [128, 64], F32)
    meanc = nc.alloc_sbuf_tensor("meanc", [128, 64], F32)
    varc = nc.alloc_sbuf_tensor("varc", [128, 64], F32)
    tmpc = nc.alloc_sbuf_tensor("tmpc", [128, 64], F32)
    rstdc = nc.alloc_sbuf_tensor("rstdc", [128, 64], F32)
    epsc = nc.alloc_sbuf_tensor("epsc", [128, 1], F32)
    m4c = nc.alloc_sbuf_tensor("m4c", [128, 1], F32)
    lamc = nc.alloc_sbuf_tensor("lamc", [128, 1], F32)
    identb = nc.alloc_sbuf_tensor("identb", [128, 128], BF16)
    onesb = nc.alloc_sbuf_tensor("onesb", [1, 128], BF16)
    onesf = nc.alloc_sbuf_tensor("onesf_sb", [128, 128], F32)
    avsb = nc.alloc_sbuf_tensor("avsb", [128, 2048], F32)
    scr4 = nc.alloc_sbuf_tensor("scr4", [128, 1024], F32)

    def carve(arena, name, shape, dtype, off_bytes):
        per_part = int(np.prod(shape[1:])) * mybir.dt.size(dtype)
        arena_bytes = int(np.prod(arena.shape[1:])) * mybir.dt.size(arena.dtype)
        assert off_bytes + per_part <= arena_bytes, (name, off_bytes, per_part, arena_bytes)
        base = nc.lookup_mloc(arena).addr
        return nc.alloc_sbuf_tensor_at(name, list(shape), dtype, offset=base + off_bytes)

    xr = carve(regA, "xr", [128, NCH * 2048], BF16, 0)
    xk_lo = carve(regA, "xk_lo", [128, 6 * 2048], BF16, 32768)
    xk_hi = carve(regA, "xk_hi", [128, 2 * 2048], BF16, 0)
    qbuf = [carve(regA, f"qbuf{p}", [128, 2048], BF16, p * 12352) for p in (0, 1)]
    kbuf = [carve(regA, f"kbuf{p}", [128, 2048], BF16, p * 12352 + 4096) for p in (0, 1)]
    vbuf = [carve(regA, f"vbuf{p}", [128, KT * 130], BF16, p * 12352 + 8192) for p in (0, 1)]
    aout = carve(regA, "aout", [128, NCH * 2048], BF16, 24704)

    r_tm = carve(regB, "r_tm", [128, NT * 1024], BF16, 0)
    lnp = carve(regB, "lnp", [128, 2 * 1024], BF16, 0)
    dumpB = carve(regA, "dumpB", [128, 1024], F32, 0)
    attnT = [carve(regB, f"attnT{p}", [128, 2048], BF16, 8192 + p * 4096) for p in (0, 1)]
    estrip = carve(regB, "estrip", [128, KT * MASKW], BF16, 16384)
    bvrow = carve(regB, "bvrow", [1, D], BF16, 27648)
    borow = carve(regB, "borow", [1, D], BF16, 29696)

    ln_fm = carve(regC, "ln_fm", [128, NCH * 2048], BF16, 0)
    k_act = carve(regC, "k_act", [128, NCH * 2048], BF16, 0)

    kr_s = carve(scr4, "kr_s", [128, 512], F32, 0)
    prod_s = carve(scr4, "prod_s", [128, 512], F32, 2048)
    sh = carve(scr4, "sh", [128, 2048], BF16, 0)
    recip_b = carve(scr4, "recip_b", [1, 2048], BF16, 0)
    strp = [carve(avsb, f"strp{p}", [128, MASKW], F32, p * (MASKW * 4)) for p in (0, 1)]

    P = Plan()
    ev = {}
    SEMS = ["swQ", "spQ", "pe", "act", "dve"]

    def pe(fn, waits=(), inc="pe"):
        return P.op("pe", fn, waits, inc)

    def act(fn, waits=(), inc="act"):
        return P.op("act", fn, waits, inc)

    def dve(fn, waits=(), inc="dve"):
        return P.op("dve", fn, waits, inc)

    def sp_dma(fn, waits=()):
        return P.op("sp", fn, waits, inc="spQ", k=16)

    def gp_dma(fn, waits=()):
        return P.op("gp", fn, waits, inc="swQ", k=16)

    wload = [0]
    wfree = {}
    wcum = {}

    def wslot(i):
        return wbuf.ap()[:, (i % 2) * 2048:(i % 2) * 2048 + 2048]

    def load_w(name, idx):
        i = wload[0]
        par = i % 2
        waits = []
        if par in wfree:
            waits.append(("pe", wfree[par]))
        wcum[i] = gp_dma(lambda e, n=name, q=idx, ii=i: e.dma_start(
            out=(wslot(ii) if n in ("wr", "wk", "wv", "wo")
                 else wslot(ii)[:, 0:1024]),
            in_=w_d[n][q]), waits=waits)
        wload[0] += 1
        return i

    x_r = x_d.rearrange("(t p) d -> p t d", p=128)
    res_r = res.ap().rearrange("p (t d) -> p t d", t=NT)
    ev["x_in"] = sp_dma(lambda e: e.dma_start(out=res_r, in_=x_r))
    for (dst, src) in [(bqp, bq_d), (bkp, bk_d), (mur, mur_d), (muk, muk_d),
                       (lamc, lam_d), (onesf, onesf_d)]:
        ev["vecs"] = sp_dma(lambda e, d=dst, s=src: e.dma_start(out=d[:], in_=s))
    ev["cast0"] = gp_dma(lambda e: e.dma_start(out=identb[:], in_=ident_d))
    ev["cast0"] = gp_dma(lambda e: e.dma_start(out=onesb[:], in_=onesf_d[0:1, :]))

    dve(lambda e: e.memset(epsc[:], EPS))
    dve(lambda e: e.memset(sumc[:], 1.0))
    dve(lambda e: e.memset(sqc[:], 2.0))
    ev["consts"] = dve(lambda e: e.memset(m4c[:], -4.0))
    dve(lambda e: e.tensor_scalar(out=murm[:], in0=mur[:], scalar1=-1.0, scalar2=1.0,
                                  op0=ALU.mult, op1=ALU.add),
        waits=[("spQ", ev["vecs"])])
    ev["muready"] = dve(lambda e: e.tensor_scalar(out=mukm[:], in0=muk[:], scalar1=-1.0,
                                                  scalar2=1.0, op0=ALU.mult, op1=ALU.add))

    ln_idx = [0]

    def emit_ln(ps_tr, tr_extra=None):
        li = ln_idx[0]
        ln_idx[0] += 1
        for tt in range(NT):
            if ("res_ready", tt) in ev:
                w = [("dve", ev[("res_ready", tt)])]
            else:
                w = [("spQ", ev["x_in"])]
            act(lambda e, t=tt: e.activation(out=dumpB.ap()[:, :],
                                             in_=res.ap()[:, t * 1024:(t + 1) * 1024],
                                             func=AF.Identity, scale=1.0,
                                             accum_out=sumc[:, t:t + 1]), waits=w)
            act(lambda e, t=tt: e.activation(out=dumpB.ap()[:, :],
                                             in_=res.ap()[:, t * 1024:(t + 1) * 1024],
                                             func=AF.Square, scale=1.0,
                                             accum_out=sqc[:, t:t + 1]))
        ev["stats"] = P.n("act")
        dve(lambda e: e.tensor_copy(out=dumpB.ap()[:, 0:512], in_=dumpB.ap()[:, 512:1024]),
            waits=[("act", ev["stats"])])
        dve(lambda e: e.tensor_scalar(out=meanc[:], in0=sumc[:], scalar1=1.0 / D,
                                      scalar2=None, op0=ALU.mult))
        dve(lambda e: e.tensor_scalar(out=varc[:], in0=sqc[:], scalar1=1.0 / D,
                                      scalar2=None, op0=ALU.mult))
        dve(lambda e: e.tensor_mul(out=tmpc[:], in0=meanc[:], in1=meanc[:]))
        dve(lambda e: e.tensor_sub(out=varc[:], in0=varc[:], in1=tmpc[:]))
        ev["var"] = dve(lambda e: e.tensor_copy(out=dumpB.ap()[:, 0:512],
                                                in_=dumpB.ap()[:, 512:1024]))
        act(lambda e: e.activation(out=tmpc[:], in_=varc[:], func=AF.Ln,
                                   bias=epsc[:, 0:1], scale=1.0),
            waits=[("dve", ev["var"])])
        act(lambda e: e.activation(out=rstdc[:], in_=tmpc[:], func=AF.Exp,
                                   scale=-0.5))
        ev["rstd"] = act(lambda e: e.activation(out=dumpB.ap()[:, 0:256],
                                                in_=dumpB.ap()[:, 0:256],
                                                func=AF.Identity, scale=1.0))
        for tt in range(NT):
            par = tt % 2
            w = [("act", ev["rstd"])]
            if ("lnt", li, tt - 2) in ev:
                w.append(("pe", ev[("lnt", li, tt - 2)]))
            ev[("lnn", li, tt)] = dve(
                lambda e, t=tt, p=par: e.tensor_scalar(
                    out=lnp.ap()[:, p * 1024:(p + 1) * 1024],
                    in0=res.ap()[:, t * 1024:(t + 1) * 1024],
                    scalar1=meanc[:, t:t + 1], scalar2=rstdc[:, t:t + 1],
                    op0=ALU.subtract, op1=ALU.mult), waits=w)
            w = [("dve", ev[("lnn", li, tt)]), ("swQ", ev["cast0"])]
            if tt == 0 and tr_extra is not None:
                w.append(("dve", tr_extra))
            if ("lne", li, tt - 2) in ev:
                w.append(("dve", ev[("lne", li, tt - 2)]))
            for g in range(NCH):
                ev[("lnt", li, tt)] = pe(
                    lambda e, gg=g, p=par: e.transpose(
                        ps_tr[p].ap()[:, gg * 128:(gg + 1) * 128],
                        lnp.ap()[:, p * 1024 + gg * 128:p * 1024 + (gg + 1) * 128],
                        identb[:]), waits=(w if g == 0 else ()))
            ev[("lne", li, tt)] = dve(
                lambda e, t=tt, p=par: e.tensor_copy(
                    out=ln_fm.ap().rearrange("p (g n) -> p g n", g=NCH)[:, :, t * 128:(t + 1) * 128],
                    in_=ps_tr[p].ap().rearrange("p (g n) -> p g n", g=NCH)),
                waits=[("pe", ev[("lnt", li, tt)])])
        ev["ln_done"] = P.n("dve")

    ffn_idx = [0]

    def emit_ffn(ps_mm):
        fi = ffn_idx[0]
        ffn_idx[0] += 1
        lnd = ev["ln_done"]

        def mix(g, mu_t, mum_t, out_ap_fn, wextra=()):
            wl = [("dve", lnd), ("dve", ev["muready"])] + list(wextra)
            dve(lambda e, gg=g, m=mum_t: e.tensor_scalar(
                out=sh.ap()[:, 1:2048],
                in0=ln_fm.ap()[:, gg * 2048:gg * 2048 + 2047],
                scalar1=m[:, gg:gg + 1], scalar2=None, op0=ALU.mult), waits=wl)
            dve(lambda e, gg=g, m=mu_t, o=out_ap_fn: e.scalar_tensor_tensor(
                out=o()[:, 1:2048], in0=ln_fm.ap()[:, gg * 2048 + 1:(gg + 1) * 2048],
                scalar=m[:, gg:gg + 1], in1=sh.ap()[:, 1:2048],
                op0=ALU.mult, op1=ALU.add))
            return dve(lambda e, gg=g, m=mu_t, o=out_ap_fn: e.tensor_scalar(
                out=o()[:, 0:1], in0=ln_fm.ap()[:, gg * 2048:gg * 2048 + 1],
                scalar1=m[:, gg:gg + 1], scalar2=None, op0=ALU.mult))

        def xk_ap(g):
            return (xk_lo.ap()[:, g * 2048:(g + 1) * 2048] if g < 6
                    else xk_hi.ap()[:, (g - 6) * 2048:(g - 5) * 2048])

        for g in range(NCH):
            ev[("xr", g)] = mix(g, mur, murm,
                                lambda gg=g: xr.ap()[:, gg * 2048:(gg + 1) * 2048])
        for g in range(6):
            ev[("xk", g)] = mix(g, muk, mukm, lambda gg=g: xk_ap(gg))

        for q4 in range(NQ4):
            wi = load_w("wr", q4)
            for tt in range(NT):
                grp = (q4 * NT + tt) % 3
                w = [("swQ", wcum[wi]), ("dve", ev[("xr", NCH - 1)])]
                if ("rev", fi, q4 * NT + tt - 3) in ev:
                    w.append(("act", ev[("rev", fi, q4 * NT + tt - 3)]))
                for g in range(NCH):
                    ev[("rmm", fi, q4, tt)] = pe(
                        lambda e, t=tt, gg=g, gr=grp, ii=wi: e.matmul(
                            ps_mm.ap()[:, gr * 512:gr * 512 + 256],
                            xr.ap()[:, gg * 2048 + t * 128:gg * 2048 + (t + 1) * 128],
                            wslot(ii)[:, gg * 256:(gg + 1) * 256],
                            start=(gg == 0), stop=(gg == NCH - 1)),
                        waits=(w if g == 0 else ()))
                ev[("rev", fi, q4 * NT + tt)] = act(
                    lambda e, t=tt, q=q4, gr=grp: e.activation(
                        out=r_tm.ap()[:, t * 1024 + q * 256:t * 1024 + (q + 1) * 256],
                        in_=ps_mm.ap()[:, gr * 512:gr * 512 + 256],
                        func=AF.Tanh, scale=0.5),
                    waits=[("pe", ev[("rmm", fi, q4, tt)])])
            wfree[wi % 2] = P.n("pe")
        ev["r_done_pe"] = P.n("pe")
        for g in (6, 7):
            ev[("xk", g)] = mix(g, muk, mukm, lambda gg=g: xk_ap(gg),
                                wextra=[("pe", ev["r_done_pe"])])
        ev["rp1"] = dve(lambda e: e.tensor_scalar(
            out=r_tm.ap()[:, :], in0=r_tm.ap()[:, :], scalar1=1.0, scalar2=None,
            op0=ALU.add), waits=[("act", ev[("rev", fi, NQ4 * NT - 1)])])

        for q4 in range(NQ4):
            wi = load_w("wk", q4)
            for blk in range(2):
                dch = q4 * 2 + blk
                for qq in range(4):
                    grp = (dch * 4 + qq) % 3
                    w = [("swQ", wcum[wi]), ("dve", ev[("xk", 7)])]
                    if ("kev", fi, dch * 4 + qq - 3) in ev:
                        w.append(("dve", ev[("kev", fi, dch * 4 + qq - 3)]))
                    for g in range(NCH):
                        ev[("kmm", fi, dch, qq)] = pe(
                            lambda e, gg=g, b=blk, t=qq, gr=grp, ii=wi: e.matmul(
                                ps_mm.ap()[:, gr * 512:(gr + 1) * 512],
                                wslot(ii)[:, gg * 256 + b * 128:gg * 256 + (b + 1) * 128],
                                xk_ap(gg)[:, t * 512:(t + 1) * 512],
                                start=(gg == 0), stop=(gg == NCH - 1)),
                            waits=(w if g == 0 else ()))
                    ev[("krl", fi, dch, qq)] = act(
                        lambda e, gr=grp: e.activation(
                            out=kr_s.ap()[:, :],
                            in_=ps_mm.ap()[:, gr * 512:(gr + 1) * 512],
                            func=AF.Relu, scale=1.0),
                        waits=[("pe", ev[("kmm", fi, dch, qq)])])
                    ev[("kev", fi, dch * 4 + qq)] = dve(
                        lambda e, d=dch, t=qq: e.tensor_mul(
                            out=k_act.ap()[:, d * 2048 + t * 512:d * 2048 + (t + 1) * 512],
                            in0=kr_s.ap()[:, :], in1=kr_s.ap()[:, :]),
                        waits=[("act", ev[("krl", fi, dch, qq)])])
            wfree[wi % 2] = P.n("pe")

        for q4 in range(NQ4):
            wi = load_w("wv", q4)
            for tt in range(NT):
                grp = (q4 * NT + tt) % 3
                w = [("swQ", wcum[wi]), ("dve", ev[("kev", fi, 31)])]
                if ("vadd", fi, q4 * NT + tt - 3) in ev:
                    w.append(("dve", ev[("vadd", fi, q4 * NT + tt - 3)]))
                for g in range(NCH):
                    ev[("vmm", fi, q4, tt)] = pe(
                        lambda e, t=tt, gg=g, gr=grp, ii=wi: e.matmul(
                            ps_mm.ap()[:, gr * 512:gr * 512 + 256],
                            k_act.ap()[:, gg * 2048 + t * 128:gg * 2048 + (t + 1) * 128],
                            wslot(ii)[:, gg * 256:(gg + 1) * 256],
                            start=(gg == 0), stop=(gg == NCH - 1)),
                        waits=(w if g == 0 else ()))
                dve(lambda e, t=tt, q=q4, gr=grp: e.scalar_tensor_tensor(
                        out=prod_s.ap()[:, 0:256],
                        in0=ps_mm.ap()[:, gr * 512:gr * 512 + 256], scalar=0.25,
                        in1=r_tm.ap()[:, t * 1024 + q * 256:t * 1024 + (q + 1) * 256],
                        op0=ALU.mult, op1=ALU.mult),
                    waits=[("pe", ev[("vmm", fi, q4, tt)]), ("dve", ev["rp1"])])
                ev[("vadd", fi, q4 * NT + tt)] = dve(
                    lambda e, t=tt, q=q4: e.tensor_add(
                        out=res.ap()[:, t * 1024 + q * 256:t * 1024 + (q + 1) * 256],
                        in0=res.ap()[:, t * 1024 + q * 256:t * 1024 + (q + 1) * 256],
                        in1=prod_s.ap()[:, 0:256]))
                if q4 == NQ4 - 1:
                    ev[("res_ready", tt)] = dve(
                        lambda e: e.tensor_copy(out=dumpB.ap()[:, 64:128],
                                                in_=dumpB.ap()[:, 0:64]))
            wfree[wi % 2] = P.n("pe")

    def emit_attn(ps_S, ps_av, ps_pr):
        lnd = ev["ln_done"]
        ev["brow"] = gp_dma(lambda e: e.dma_start(out=bvrow[:], in_=bv_d),
                            waits=[("act", ev["stats"])])
        ev["brow"] = gp_dma(lambda e: e.dma_start(out=borow[:], in_=bo_d))
        for kt in range(KT):
            par = kt % 2
            w = []
            if ("strE", kt - 2) in ev:
                w.append(("act", ev[("strE", kt - 2)]))
            ev[("strD", kt)] = sp_dma(
                lambda e, k=kt, p=par: e.dma_start(out=strp[p].ap()[:, :], in_=strips_d[k]),
                waits=w)
            ev[("strE", kt)] = act(
                lambda e, k=kt, p=par: e.activation(
                    out=estrip.ap()[:, k * MASKW:(k + 1) * MASKW],
                    in_=strp[p].ap()[:, :], func=AF.Exp, scale=lamc[:, 0:1]),
                waits=[("spQ", ev[("strD", kt)]), ("spQ", ev["vecs"])])
        ev["strips"] = P.n("act")


        class ProjQ:
            def __init__(self, hp):
                self.hp = hp
                self.par = hp % 2
                self.j = 0
                self.wiq = load_w("wq", hp)
                self.wik = load_w("wka", hp)
                self.wiv = None
                dve(lambda e, p=self.par: e.memset(vbuf[p].ap()[:, :], 1.0),
                    waits=[("dve", ev["ln_done"]),
                           ("pe", ev.get(("pair_gone", hp - 2), 0))])
                ev[("vset", hp)] = P.n("dve")

            def pending(self):
                return self.j < 24

            def emit_one(self):
                j, hp, par = self.j, self.hp, self.par
                self.j += 1
                bank = j % 2
                sl = bank * 512
                war = []
                if ("prev", j - 2) in ev:
                    pass
                if j == 4:
                    wfree[self.wiq % 2] = P.n("pe")
                    self.wiv = load_w("wva", hp)
                if j < 8:
                    nm, wi, buf, bias = (("q", self.wiq, qbuf[par], bqp) if j < 4
                                         else ("k", self.wik, kbuf[par], bkp))
                    qq = j % 4
                    w = [("swQ", wcum[wi]), ("dve", ev["ln_done"]),
                         ("dve", ev.get("pr_nmul", 0)),
                         ("pe", ev.get(("pair_gone", hp - 2), 0))]
                    if ("prev", hp, j - 2) in ev:
                        w.append(("act", ev[("prev", hp, j - 2)]))
                    for g in range(NCH):
                        ev[("prmm", hp, j)] = pe(
                            lambda e, gg=g, t=qq, ii=wi, s2=sl: e.matmul(
                                ps_pr.ap()[:, s2:s2 + 512],
                                wslot(ii)[:, gg * 128:(gg + 1) * 128],
                                ln_fm.ap()[:, gg * 2048 + t * 512:gg * 2048 + (t + 1) * 512],
                                start=(gg == 0), stop=(gg == NCH - 1)),
                            waits=(w if g == 0 else ()))
                    ev[("prev", hp, j)] = act(
                        lambda e, b=buf, t=qq, bi=bias, hh=hp, s2=sl: e.activation(
                            out=b.ap()[:, t * 512:(t + 1) * 512],
                            in_=ps_pr.ap()[:, s2:s2 + 512],
                            func=AF.Identity, bias=bi[:, hh:hh + 1], scale=1.0),
                        waits=[("pe", ev[("prmm", hp, j)])])
                    self.last_act = ev[("prev", hp, j)]
                    if j == 7:
                        ev[("qk_evac", hp)] = P.n("act")
                else:
                    kt = j - 8
                    w = [("swQ", wcum[self.wiv]), ("swQ", ev["brow"]),
                         ("act", ev[("qk_evac", hp)]),
                         ("dve", ev.get("pr_nmul", 0))]
                    if ("prve", hp, kt - 2) in ev:
                        w.append(("dve", ev[("prve", hp, kt - 2)]))
                    pe(lambda e, k=kt, hh=hp, s2=sl: e.matmul(
                            ps_pr.ap()[:, s2:s2 + 128],
                            onesb[0:1, 0:128],
                            bvrow.ap()[0:1, hh * 128:(hh + 1) * 128],
                            start=True, stop=False), waits=w)
                    for g in range(NCH):
                        ev[("prvm", hp, kt)] = pe(
                            lambda e, k=kt, gg=g, ii=self.wiv, s2=sl: e.matmul(
                                ps_pr.ap()[:, s2:s2 + 128],
                                ln_fm.ap()[:, gg * 2048 + k * 128:gg * 2048 + (k + 1) * 128],
                                wslot(ii)[:, gg * 128:(gg + 1) * 128],
                                start=False, stop=(gg == NCH - 1)))
                    ev[("prve", hp, kt)] = dve(
                        lambda e, k=kt, p=par: e.tensor_copy(
                            out=vbuf[p].ap()[:, k * 130:k * 130 + 130]
                                .rearrange("p (h c) -> p h c", c=65)[:, :, 0:64],
                            in_=ps_pr.ap()[:, (k % 2) * 512:(k % 2) * 512 + 128]
                                .rearrange("p (h c) -> p h c", c=64)),
                        waits=[("pe", ev[("prvm", hp, kt)]), ("dve", ev[("vset", hp)])])

            def finish(self):
                while self.pending():
                    self.emit_one()
                wfree[self.wik % 2] = P.n("pe")
                wfree[self.wiv % 2] = P.n("pe")
                ev[("vproj", self.hp)] = P.n("dve")
                ev[("pr_done_act", self.hp)] = P.n("act")
                self.last = (P.n("pe"), P.n("act"), P.n("dve"))

        projq = [None]

        def emit_unit(uidx, h, qh):
            hp, s = h // 2, h % 2
            par = hp % 2
            for kt in range(KT + 1):
                if kt < KT:
                    w0 = min(max(kt * 128 - MASKR, 0), L - MASKW)
                    w = [("dve", ev[("vproj", hp)]), ("act", ev[("pr_done_act", hp)])]
                    if ("exp", uidx, kt - 2) in ev:
                        w.append(("act", ev[("exp", uidx, kt - 2)]))
                    elif ("exp", uidx - 1, 14 + kt) in ev:
                        w.append(("act", ev[("exp", uidx - 1, 14 + kt)]))
                    S0 = (kt % 2) * 1024
                    for lq in range(2):
                        ev[("smm", uidx, kt)] = pe(
                            lambda e, k=kt, q=lq, ss=s, S2=S0: e.matmul(
                                ps_S.ap()[:, S2 + q * 512:S2 + (q + 1) * 512],
                                kbuf[par].ap()[ss * 64:(ss + 1) * 64, k * 128:(k + 1) * 128],
                                qbuf[par].ap()[ss * 64:(ss + 1) * 64,
                                               qh * 1024 + q * 512:qh * 1024 + (q + 1) * 512],
                                start=True, stop=True),
                            waits=(w if lq == 0 else ()))
                    w = [("pe", ev[("smm", uidx, kt)]), ("dve", ev["consts"]),
                         ("act", ev["strips"])]
                    if ("avmq", uidx, kt - 2) in ev:
                        w.append(("pe", ev[("avmq", uidx, kt - 2)]))
                    elif ("avmq", uidx - 1, 14 + kt) in ev:
                        w.append(("pe", ev[("avmq", uidx - 1, 14 + kt)]))
                    ev[("exp", uidx, kt)] = act(
                        lambda e, k=kt, S2=S0: e.activation(
                            out=attnT[k % 2].ap()[:, qh * 1024:(qh + 1) * 1024]
                                if False else attnT[k % 2].ap()[:, 0:1024],
                            in_=ps_S.ap()[:, S2:S2 + 1024],
                            func=AF.Exp, bias=m4c[:, 0:1], scale=0.125),
                        waits=w)
                    ev[("uexp", uidx, 14 + (kt - 14) if kt >= 14 else kt)] =                         ev[("exp", uidx, kt)]
                if kt >= 1:
                    k2 = kt - 1
                    w0b = min(max(k2 * 128 - MASKR, 0), L - MASKW)
                    b0 = max(w0b, qh * 1024)
                    b1 = min(w0b + MASKW, (qh + 1) * 1024)
                    if b1 > b0:
                        ev[("band", uidx, k2)] = dve(
                            lambda e, k=k2, a2=b0, b2=b1, ww=w0b: e.tensor_mul(
                                out=attnT[k % 2].ap()[:, a2 - qh * 1024:b2 - qh * 1024],
                                in0=attnT[k % 2].ap()[:, a2 - qh * 1024:b2 - qh * 1024],
                                in1=estrip.ap()[:, k * MASKW + (a2 - ww):k * MASKW + (b2 - ww)]),
                            waits=[("act", ev[("exp", uidx, k2)])])
                    else:
                        ev[("band", uidx, k2)] = 0
                    w = [("act", ev[("exp", uidx, k2)]), ("dve", ev[("band", uidx, k2)])]
                    if k2 == 0:
                        w.append(("dve", ev.get(("uncp", uidx - 1), 0)))
                    for lq in range(2):
                        ev[("avmq", uidx, k2)] = pe(
                            lambda e, k=k2, q=lq: e.matmul(
                                ps_av.ap()[0:65, q * 512:(q + 1) * 512],
                                vbuf[par].ap()[:, k * 130 + s * 65:k * 130 + (s + 1) * 65],
                                attnT[k % 2].ap()[:, q * 512:(q + 1) * 512],
                                start=(k == 0), stop=(k == KT - 1)),
                            waits=(w if lq == 0 else ()))
                if kt == 3 and pending[0] is not None:
                    flush_norm()
                if projq[0] is not None and projq[0].pending():
                    projq[0].emit_one()
            ev[("uncp", uidx)] = dve(
                lambda e: e.tensor_copy(out=avsb.ap()[0:65, 0:1024],
                                        in_=ps_av.ap()[0:65, 0:1024]),
                waits=[("pe", ev[("avmq", uidx, KT - 1)]),
                       ("dve", ev.get("pr_nmul", 0))])
            act(lambda e: e.activation(out=avsb.ap()[64:65, 0:1024],
                                       in_=avsb.ap()[64:65, 0:1024],
                                       func=AF.Ln, scale=1.0),
                waits=[("dve", ev[("uncp", uidx)])])
            ev[("nrc", uidx)] = act(lambda e: e.activation(
                out=recip_b.ap()[0:1, 0:1024], in_=avsb.ap()[64:65, 0:1024],
                func=AF.Exp, scale=-1.0))
            pending[0] = (uidx, hp, s, qh)

        pending = [None]

        def flush_norm():
            uidx, hp, s, qh = pending[0]
            pending[0] = None
            w = [("act", ev[("nrc", uidx)]),
                 ("act", P.n("act")), ("dve", P.n("dve"))]
            for q in range(2):
                ev[("unbc", uidx)] = pe(
                    lambda e, qq=q: e.matmul(
                        ps_pr.ap()[0:64, qq * 512:(qq + 1) * 512],
                        onesb[0:1, 0:64],
                        recip_b.ap()[0:1, qq * 512:(qq + 1) * 512],
                        start=True, stop=True),
                    waits=(w if q == 0 else ()))
            ev[("unmul", uidx)] = dve(
                lambda e, ss=s, hh=hp, qq=qh: e.tensor_mul(
                    out=aout.ap()[ss * 64:(ss + 1) * 64,
                                  hh * 2048 + qq * 1024:hh * 2048 + (qq + 1) * 1024],
                    in0=avsb.ap()[0:64, 0:1024],
                    in1=ps_pr.ap()[0:64, 0:1024]),
                waits=[("pe", ev[("unbc", uidx)])])
            ev["pr_nmul"] = ev[("unmul", uidx)]

        projq[0] = ProjQ(0)
        projq[0].finish()
        uidx = 0
        for hp in range(8):
            for (h, qh) in [(2 * hp, 0), (2 * hp, 1), (2 * hp + 1, 0), (2 * hp + 1, 1)]:
                if (h, qh) == (2 * hp, 1) and hp < 7:
                    projq[0] = ProjQ(hp + 1)
                emit_unit(uidx, h, qh)
                uidx += 1
            if projq[0] is not None and projq[0].hp == hp + 1:
                projq[0].finish()
            ev[("pair_gone", hp)] = P.n("pe")
        flush_norm()
        ev[("pair_gone", 7)] = P.n("pe")
        ev["S_free_act"] = P.n("act")
        ev["attn_done"] = P.n("dve")

        for q4 in range(NQ4):
            wi = load_w("wo", q4)
            for tt in range(NT):
                grp = (q4 * NT + tt) % 4
                w = [("swQ", wcum[wi]), ("dve", ev["attn_done"]),
                     ("swQ", ev["brow"]), ("act", ev["S_free_act"])]
                if ("oadd", q4 * NT + tt - 4) in ev:
                    w.append(("dve", ev[("oadd", q4 * NT + tt - 4)]))
                pe(lambda e, q=q4, gr=grp: e.matmul(
                        ps_S.ap()[:, gr * 512:gr * 512 + 256],
                        onesb[0:1, 0:128],
                        borow.ap()[0:1, q * 256:(q + 1) * 256],
                        start=True, stop=False), waits=w)
                for g in range(NCH):
                    ev[("omm", q4, tt)] = pe(
                        lambda e, t=tt, gg=g, gr=grp, ii=wi: e.matmul(
                            ps_S.ap()[:, gr * 512:gr * 512 + 256],
                            aout.ap()[:, gg * 2048 + t * 128:gg * 2048 + (t + 1) * 128],
                            wslot(ii)[:, gg * 256:(gg + 1) * 256],
                            start=False, stop=(gg == NCH - 1)))
                ev[("oadd", q4 * NT + tt)] = dve(
                    lambda e, t=tt, q=q4, gr=grp: e.tensor_add(
                        out=res.ap()[:, t * 1024 + q * 256:t * 1024 + (q + 1) * 256],
                        in0=res.ap()[:, t * 1024 + q * 256:t * 1024 + (q + 1) * 256],
                        in1=ps_S.ap()[:, gr * 512:gr * 512 + 256]),
                    waits=[("pe", ev[("omm", q4, tt)])])
                if q4 == NQ4 - 1:
                    ev[("res_ready", tt)] = dve(
                        lambda e: e.tensor_copy(out=dumpB.ap()[:, 64:128],
                                                in_=dumpB.ap()[:, 0:64]))
            wfree[wi % 2] = P.n("pe")
        ev["o_done"] = P.n("dve")

    done = [0]

    def grp13(tag, emits):
        with ExitStack() as psx:
            ps_mm = psx.enter_context(nc.psum_tensor(f"ps_mm{tag}", [128, 1536], F32))
            ps_tr = [psx.enter_context(nc.psum_tensor(f"ps_tr{tag}{p}", [128, 1024], BF16))
                     for p in (0, 1)]
            for e in emits:
                if done[0] >= n_phases:
                    return
                if e == "ln":
                    emit_ln(ps_tr)
                elif e == "lnx":
                    emit_ln(ps_tr, tr_extra=ev["o_done"])
                else:
                    emit_ffn(ps_mm)
                done[0] += 1

    grp13("1", ["ln", "ffn", "ln"])
    if done[0] < n_phases:
        with ExitStack() as psx:
            ps_S = psx.enter_context(nc.psum_tensor("ps_S", [128, 2048], F32))
            ps_av = psx.enter_context(nc.psum_tensor("ps_av", [128, 1024], F32))
            ps_pr = psx.enter_context(nc.psum_tensor("ps_pr", [128, 1024], F32))
            emit_attn(ps_S, ps_av, ps_pr)
            done[0] += 1
    grp13("2", ["lnx", "ffn"])

    out_r = out_d.rearrange("(t p) d -> p t d", p=128)
    if n_phases >= 6:
        stw = [("dve", ev[("vadd", 1, NQ4 * NT - 1)])]
    else:
        pe(lambda e: e.nop())
        act(lambda e: e.activation(out=dumpB.ap()[:, 0:512], in_=dumpB.ap()[:, 0:512],
                                   func=AF.Identity, scale=1.0))
        dve(lambda e: e.tensor_copy(out=dumpB.ap()[:, 0:512], in_=dumpB.ap()[:, 512:1024]))
        stw = [("pe", P.n("pe")), ("act", P.n("act")), ("dve", P.n("dve"))]
    st = sp_dma(lambda e: e.dma_start(out=out_r, in_=res_r), waits=stw)
    if dbg:
        dbg_list = [("dbg_lnfm", ln_fm, BF16), ("dbg_aout", aout, BF16),
                    ("dbg_rtm", r_tm, BF16), ("dbg_kact", k_act, BF16),
                    ("dbg_avsb", avsb, F32), ("dbg_xr", xr, BF16),
                    ("dbg_q0", qbuf[0], BF16), ("dbg_k0", kbuf[0], BF16),
                    ("dbg_v0", vbuf[0], BF16)]
        for nm, t, dt_ in dbg_list:
            d = nc.dram_tensor(nm, [128, int(np.prod(t.shape[1:]))], dt_,
                               kind="ExternalOutput").ap()
            st = sp_dma(lambda e, dd=d, rr=t: e.dma_start(out=dd, in_=rr.ap()[:, :]),
                        waits=stw)
    P.op("sp", lambda e: e.nop(), waits=[("spQ", P.n("spQ"))])

    with ExitStack() as bctx:
        sems = {s: bctx.enter_context(nc.semaphore(s)) for s in SEMS}
        for s in SEMS:
            nc.gpsimd.sem_clear(sems[s])
        nc.all_engine_barrier()
        block = bctx.enter_context(nc.Block())

        @block.tensor
        def _(e):
            P.replay("pe", e, sems)

        @block.scalar
        def _(e):
            P.replay("act", e, sems)

        @block.vector
        def _(e):
            P.replay("dve", e, sems)

        @block.sync
        def _(e):
            P.replay("sp", e, sems)

        @block.gpsimd
        def _(e):
            P.replay("gp", e, sems)

    return nc


def _prep_w_quarters(w):
    wt = np.ascontiguousarray(np.asarray(w, np.float32).T)
    t = wt.reshape(NCH, 128, NQ4, 256)
    t = np.ascontiguousarray(t.transpose(2, 1, 0, 3))
    return t.reshape(NQ4, 128, NCH * 256)


def _prep_w_pairs(w):
    wt = np.ascontiguousarray(np.asarray(w, np.float32).T)
    t = wt.reshape(NCH, 128, 8, 128)
    t = np.ascontiguousarray(t.transpose(2, 1, 0, 3))
    return t.reshape(8, 128, NCH * 128)


def _prep_col(v):
    return np.ascontiguousarray(np.asarray(v, np.float32).reshape(8, 128).T)


def _make_strips():
    out = np.zeros((KT, 128, MASKW), dtype=np.float64)
    p = np.arange(128)
    j = np.arange(MASKW)
    for kt in range(KT):
        w0 = min(max(kt * 128 - MASKR, 0), L - MASKW)
        k = kt * 128 + p
        delta = np.abs(k[:, None] - (w0 + j)[None, :]).astype(np.float64)
        out[kt] = 0.8 * np.exp(-((delta - 1.0) ** 2) / 0.5) + \
            0.4 * np.exp(-((delta - 30.0) ** 2) / 8.0)
    return out.astype(np.float32)


_NC_CACHE = {}


def kernel(L0_attn_x, L0_attn_Wq, L0_attn_bq, L0_attn_Wk, L0_attn_bk,
           L0_attn_Wv, L0_attn_bv, L0_attn_Wo, L0_attn_bo,
           L0_attn_prior_lambda, L0_ffn_W_r, L0_ffn_W_k, L0_ffn_W_v,
           L0_ffn_mu_r, L0_ffn_mu_k):
    x = np.ascontiguousarray(np.asarray(L0_attn_x, dtype=np.float32))
    B = x.shape[0]
    assert x.shape == (B, L, D)

    common = dict(
        wr=_prep_w_quarters(L0_ffn_W_r),
        wk=_prep_w_quarters(L0_ffn_W_k),
        wv=_prep_w_quarters(L0_ffn_W_v),
        wo=_prep_w_quarters(L0_attn_Wo),
        wq=_prep_w_pairs(L0_attn_Wq),
        wka=_prep_w_pairs(L0_attn_Wk),
        wva=_prep_w_pairs(L0_attn_Wv),
        bq=_prep_col(L0_attn_bq),
        bk=_prep_col(L0_attn_bk),
        bv=np.ascontiguousarray(np.asarray(L0_attn_bv, np.float32).reshape(1, D)),
        bo=np.ascontiguousarray(np.asarray(L0_attn_bo, np.float32).reshape(1, D)),
        mur=_prep_col(L0_ffn_mu_r),
        muk=_prep_col(L0_ffn_mu_k),
        lam=np.ascontiguousarray(np.broadcast_to(np.asarray(L0_attn_prior_lambda, np.float32).reshape(1, 1), (128, 1))),
        strips=_make_strips(),
        ident=np.eye(128, dtype=np.float32),
        onesf=np.ones((128, 128), dtype=np.float32),
    )

    if "nc" not in _NC_CACHE:
        _NC_CACHE["nc"] = build_nc()
    nc = _NC_CACHE["nc"]

    in_maps = [dict(common, x=np.ascontiguousarray(x[b])) for b in range(B)]
    res = run_bass_kernel_spmd(nc, in_maps, core_ids=list(range(B)))
    return np.stack([r["out"] for r in res.results], axis=0)


if __name__ == "__main__":
    nc = build_nc()
    print("built:", nc.n_instructions(), "instructions")


# version 15
# speedup vs baseline: 1.0754x; 1.0754x over previous
import numpy as np
from contextlib import ExitStack

import concourse.bass as bass
import concourse.mybir as mybir
from concourse.bass_utils import run_bass_kernel_spmd

F32 = mybir.dt.float32
BF16 = mybir.dt.bfloat16
U8 = mybir.dt.uint8
AF = mybir.ActivationFunctionType
ALU = mybir.AluOpType

L = 2048
D = 1024
NT = 16
NCH = 8
NQ4 = 4
KT = 16
MASKW = 352
MASKR = 112
EPS = 1e-5


class Plan:
    def __init__(self):
        self.steps = []
        self.cnt = {}

    def op(self, eng, fn, waits=(), inc=None, k=1):
        for (s, v) in waits:
            if v is not None:
                assert v <= self.cnt.get(s, 0), (s, v, self.cnt.get(s, 0))
        c = None
        if inc is not None:
            self.cnt[inc] = self.cnt.get(inc, 0) + k
            c = self.cnt[inc]
        self.steps.append((eng, tuple(waits), fn, inc, k))
        return c

    def n(self, sem):
        return self.cnt.get(sem, 0)

    def replay(self, eng_key, handle, sems):
        waited = {}
        for eng, waits, fn, inc, k in self.steps:
            if eng != eng_key:
                continue
            for (s, v) in waits:
                if v is None or v <= 0 or waited.get(s, -1) >= v:
                    continue
                handle.wait_ge(sems[s], v)
                waited[s] = v
            inst = fn(handle)
            if inc is not None:
                inst.then_inc(sems[inc], k)


def build_nc(n_phases=6, dbg=False):
    nc = bass.Bass("TRN2", target_bir_lowering=False, debug=False)

    x_d = nc.dram_tensor("x", [L, D], F32, kind="ExternalInput").ap()
    w_d = {}
    for n in ["wr", "wk", "wv", "wo"]:
        w_d[n] = nc.dram_tensor(n, [NQ4, 128, NCH * 256], F32, kind="ExternalInput").ap()
    for n in ["wq", "wka", "wva"]:
        w_d[n] = nc.dram_tensor(n, [8, 128, NCH * 128], F32, kind="ExternalInput").ap()
    bq_d = nc.dram_tensor("bq", [128, 8], F32, kind="ExternalInput").ap()
    bk_d = nc.dram_tensor("bk", [128, 8], F32, kind="ExternalInput").ap()
    bv_d = nc.dram_tensor("bv", [1, D], F32, kind="ExternalInput").ap()
    bo_d = nc.dram_tensor("bo", [1, D], F32, kind="ExternalInput").ap()
    mur_d = nc.dram_tensor("mur", [128, 8], F32, kind="ExternalInput").ap()
    muk_d = nc.dram_tensor("muk", [128, 8], F32, kind="ExternalInput").ap()
    lam_d = nc.dram_tensor("lam", [128, 1], F32, kind="ExternalInput").ap()
    strips_d = nc.dram_tensor("strips", [KT, 128, MASKW], F32, kind="ExternalInput").ap()
    ident_d = nc.dram_tensor("ident", [128, 128], F32, kind="ExternalInput").ap()
    onesf_d = nc.dram_tensor("onesf", [128, 128], F32, kind="ExternalInput").ap()
    out_d = nc.dram_tensor("out", [L, D], F32, kind="ExternalOutput").ap()

    res = nc.alloc_sbuf_tensor("res", [128, NT * 1024], F32)
    wbuf = nc.alloc_sbuf_tensor("wbuf", [128, 2 * 2048], BF16)
    regA = nc.alloc_sbuf_tensor("regA", [128, 57472], U8)
    regB = nc.alloc_sbuf_tensor("regB", [128, 32768], U8)
    regC = nc.alloc_sbuf_tensor("regC", [128, 32768], U8)
    bqp = nc.alloc_sbuf_tensor("bqp", [128, 8], F32)
    bkp = nc.alloc_sbuf_tensor("bkp", [128, 8], F32)
    mur = nc.alloc_sbuf_tensor("mur_sb", [128, 8], F32)
    muk = nc.alloc_sbuf_tensor("muk_sb", [128, 8], F32)
    murm = nc.alloc_sbuf_tensor("murm", [128, 8], F32)
    mukm = nc.alloc_sbuf_tensor("mukm", [128, 8], F32)
    sumc = nc.alloc_sbuf_tensor("sumc", [128, 64], F32)
    sqc = nc.alloc_sbuf_tensor("sqc", [128, 64], F32)
    meanc = nc.alloc_sbuf_tensor("meanc", [128, 64], F32)
    varc = nc.alloc_sbuf_tensor("varc", [128, 64], F32)
    tmpc = nc.alloc_sbuf_tensor("tmpc", [128, 64], F32)
    rstdc = nc.alloc_sbuf_tensor("rstdc", [128, 64], F32)
    epsc = nc.alloc_sbuf_tensor("epsc", [128, 1], F32)
    m4c = nc.alloc_sbuf_tensor("m4c", [128, 1], F32)
    lamc = nc.alloc_sbuf_tensor("lamc", [128, 1], F32)
    identb = nc.alloc_sbuf_tensor("identb", [128, 128], BF16)
    onesb = nc.alloc_sbuf_tensor("onesb", [1, 128], BF16)
    onesf = nc.alloc_sbuf_tensor("onesf_sb", [128, 128], F32)
    avsb = nc.alloc_sbuf_tensor("avsb", [128, 2048], F32)
    scr4 = nc.alloc_sbuf_tensor("scr4", [128, 1024], F32)

    def carve(arena, name, shape, dtype, off_bytes):
        per_part = int(np.prod(shape[1:])) * mybir.dt.size(dtype)
        arena_bytes = int(np.prod(arena.shape[1:])) * mybir.dt.size(arena.dtype)
        assert off_bytes + per_part <= arena_bytes, (name, off_bytes, per_part, arena_bytes)
        base = nc.lookup_mloc(arena).addr
        return nc.alloc_sbuf_tensor_at(name, list(shape), dtype, offset=base + off_bytes)

    xr = carve(regA, "xr", [128, NCH * 2048], BF16, 0)
    xk_lo = carve(regA, "xk_lo", [128, 6 * 2048], BF16, 32768)
    xk_hi = carve(regA, "xk_hi", [128, 2 * 2048], BF16, 0)
    qbuf = [carve(regA, f"qbuf{p}", [128, 2048], BF16, p * 12352) for p in (0, 1)]
    kbuf = [carve(regA, f"kbuf{p}", [128, 2048], BF16, p * 12352 + 4096) for p in (0, 1)]
    vbuf = [carve(regA, f"vbuf{p}", [128, KT * 130], BF16, p * 12352 + 8192) for p in (0, 1)]
    aout = carve(regA, "aout", [128, NCH * 2048], BF16, 24704)

    r_tm = carve(regB, "r_tm", [128, NT * 1024], BF16, 0)
    lnp = carve(regB, "lnp", [128, 2 * 1024], BF16, 0)
    dumpB = carve(regA, "dumpB", [128, 1024], F32, 0)
    attnT = [carve(regB, f"attnT{p}", [128, 2048], BF16, 8192 + p * 4096) for p in (0, 1)]
    estrip = carve(regB, "estrip", [128, KT * MASKW], BF16, 16384)
    bvrow = carve(regB, "bvrow", [1, D], BF16, 27648)
    borow = carve(regB, "borow", [1, D], BF16, 29696)

    ln_fm = carve(regC, "ln_fm", [128, NCH * 2048], BF16, 0)
    k_act = carve(regC, "k_act", [128, NCH * 2048], BF16, 0)

    kr_s = carve(scr4, "kr_s", [128, 512], F32, 0)
    prod_s = carve(scr4, "prod_s", [128, 512], F32, 2048)
    sh = carve(scr4, "sh", [128, 2048], BF16, 0)
    recip_b = carve(scr4, "recip_b", [1, 2048], BF16, 0)
    strp = [carve(avsb, f"strp{p}", [128, MASKW], F32, p * (MASKW * 4)) for p in (0, 1)]

    P = Plan()
    ev = {}
    SEMS = ["swQ", "spQ", "pe", "act", "dve"]

    def pe(fn, waits=(), inc="pe"):
        return P.op("pe", fn, waits, inc)

    def act(fn, waits=(), inc="act"):
        return P.op("act", fn, waits, inc)

    def dve(fn, waits=(), inc="dve"):
        return P.op("dve", fn, waits, inc)

    def sp_dma(fn, waits=()):
        return P.op("sp", fn, waits, inc="spQ", k=16)

    def gp_dma(fn, waits=()):
        return P.op("gp", fn, waits, inc="swQ", k=16)

    wload = [0]
    wfree = {}
    wcum = {}

    def wslot(i):
        return wbuf.ap()[:, (i % 2) * 2048:(i % 2) * 2048 + 2048]

    def load_w(name, idx):
        i = wload[0]
        par = i % 2
        waits = []
        if par in wfree:
            waits.append(("pe", wfree[par]))
        wcum[i] = gp_dma(lambda e, n=name, q=idx, ii=i: e.dma_start(
            out=(wslot(ii) if n in ("wr", "wk", "wv", "wo")
                 else wslot(ii)[:, 0:1024]),
            in_=w_d[n][q]), waits=waits)
        wload[0] += 1
        return i

    x_r = x_d.rearrange("(t p) d -> p t d", p=128)
    res_r = res.ap().rearrange("p (t d) -> p t d", t=NT)
    ev["x_in"] = sp_dma(lambda e: e.dma_start(out=res_r, in_=x_r))
    for (dst, src) in [(bqp, bq_d), (bkp, bk_d), (mur, mur_d), (muk, muk_d),
                       (lamc, lam_d), (onesf, onesf_d)]:
        ev["vecs"] = sp_dma(lambda e, d=dst, s=src: e.dma_start(out=d[:], in_=s))
    ev["cast0"] = gp_dma(lambda e: e.dma_start(out=identb[:], in_=ident_d))
    ev["cast0"] = gp_dma(lambda e: e.dma_start(out=onesb[:], in_=onesf_d[0:1, :]))

    dve(lambda e: e.memset(epsc[:], EPS))
    dve(lambda e: e.memset(sumc[:], 1.0))
    dve(lambda e: e.memset(sqc[:], 2.0))
    ev["consts"] = dve(lambda e: e.memset(m4c[:], -4.0))
    dve(lambda e: e.tensor_scalar(out=murm[:], in0=mur[:], scalar1=-1.0, scalar2=1.0,
                                  op0=ALU.mult, op1=ALU.add),
        waits=[("spQ", ev["vecs"])])
    ev["muready"] = dve(lambda e: e.tensor_scalar(out=mukm[:], in0=muk[:], scalar1=-1.0,
                                                  scalar2=1.0, op0=ALU.mult, op1=ALU.add))

    ln_idx = [0]

    def emit_ln(ps_tr, tr_extra=None):
        li = ln_idx[0]
        ln_idx[0] += 1
        for tt in range(NT):
            if ("res_ready", tt) in ev:
                w = [("dve", ev[("res_ready", tt)])]
            else:
                w = [("spQ", ev["x_in"])]
            act(lambda e, t=tt: e.activation(out=dumpB.ap()[:, :],
                                             in_=res.ap()[:, t * 1024:(t + 1) * 1024],
                                             func=AF.Identity, scale=1.0,
                                             accum_out=sumc[:, t:t + 1]), waits=w)
            act(lambda e, t=tt: e.activation(out=dumpB.ap()[:, :],
                                             in_=res.ap()[:, t * 1024:(t + 1) * 1024],
                                             func=AF.Square, scale=1.0,
                                             accum_out=sqc[:, t:t + 1]))
        ev["stats"] = P.n("act")
        dve(lambda e: e.tensor_copy(out=dumpB.ap()[:, 0:512], in_=dumpB.ap()[:, 512:1024]),
            waits=[("act", ev["stats"])])
        dve(lambda e: e.tensor_scalar(out=meanc[:], in0=sumc[:], scalar1=1.0 / D,
                                      scalar2=None, op0=ALU.mult))
        dve(lambda e: e.tensor_scalar(out=varc[:], in0=sqc[:], scalar1=1.0 / D,
                                      scalar2=None, op0=ALU.mult))
        dve(lambda e: e.tensor_mul(out=tmpc[:], in0=meanc[:], in1=meanc[:]))
        dve(lambda e: e.tensor_sub(out=varc[:], in0=varc[:], in1=tmpc[:]))
        ev["var"] = dve(lambda e: e.tensor_copy(out=dumpB.ap()[:, 0:512],
                                                in_=dumpB.ap()[:, 512:1024]))
        act(lambda e: e.activation(out=tmpc[:], in_=varc[:], func=AF.Ln,
                                   bias=epsc[:, 0:1], scale=1.0),
            waits=[("dve", ev["var"])])
        act(lambda e: e.activation(out=rstdc[:], in_=tmpc[:], func=AF.Exp,
                                   scale=-0.5))
        ev["rstd"] = act(lambda e: e.activation(out=dumpB.ap()[:, 0:256],
                                                in_=dumpB.ap()[:, 0:256],
                                                func=AF.Identity, scale=1.0))
        for tt in range(NT):
            par = tt % 2
            w = [("act", ev["rstd"])]
            if ("lnt", li, tt - 2) in ev:
                w.append(("pe", ev[("lnt", li, tt - 2)]))
            ev[("lnn", li, tt)] = dve(
                lambda e, t=tt, p=par: e.tensor_scalar(
                    out=lnp.ap()[:, p * 1024:(p + 1) * 1024],
                    in0=res.ap()[:, t * 1024:(t + 1) * 1024],
                    scalar1=meanc[:, t:t + 1], scalar2=rstdc[:, t:t + 1],
                    op0=ALU.subtract, op1=ALU.mult), waits=w)
            w = [("dve", ev[("lnn", li, tt)]), ("swQ", ev["cast0"])]
            if tt == 0 and tr_extra is not None:
                w.append(("dve", tr_extra))
            if ("lne", li, tt - 2) in ev:
                w.append(("dve", ev[("lne", li, tt - 2)]))
            for g in range(NCH):
                ev[("lnt", li, tt)] = pe(
                    lambda e, gg=g, p=par: e.transpose(
                        ps_tr[p].ap()[:, gg * 128:(gg + 1) * 128],
                        lnp.ap()[:, p * 1024 + gg * 128:p * 1024 + (gg + 1) * 128],
                        identb[:]), waits=(w if g == 0 else ()))
            ev[("lne", li, tt)] = dve(
                lambda e, t=tt, p=par: e.tensor_copy(
                    out=ln_fm.ap().rearrange("p (g n) -> p g n", g=NCH)[:, :, t * 128:(t + 1) * 128],
                    in_=ps_tr[p].ap().rearrange("p (g n) -> p g n", g=NCH)),
                waits=[("pe", ev[("lnt", li, tt)])])
        ev["ln_done"] = P.n("dve")

    ffn_idx = [0]

    def emit_ffn(ps_mm):
        fi = ffn_idx[0]
        ffn_idx[0] += 1
        lnd = ev["ln_done"]

        def mix(g, mu_t, mum_t, out_ap_fn, wextra=()):
            wl = [("dve", lnd), ("dve", ev["muready"])] + list(wextra)
            dve(lambda e, gg=g, m=mum_t: e.tensor_scalar(
                out=sh.ap()[:, 1:2048],
                in0=ln_fm.ap()[:, gg * 2048:gg * 2048 + 2047],
                scalar1=m[:, gg:gg + 1], scalar2=None, op0=ALU.mult), waits=wl)
            dve(lambda e, gg=g, m=mu_t, o=out_ap_fn: e.scalar_tensor_tensor(
                out=o()[:, 1:2048], in0=ln_fm.ap()[:, gg * 2048 + 1:(gg + 1) * 2048],
                scalar=m[:, gg:gg + 1], in1=sh.ap()[:, 1:2048],
                op0=ALU.mult, op1=ALU.add))
            return dve(lambda e, gg=g, m=mu_t, o=out_ap_fn: e.tensor_scalar(
                out=o()[:, 0:1], in0=ln_fm.ap()[:, gg * 2048:gg * 2048 + 1],
                scalar1=m[:, gg:gg + 1], scalar2=None, op0=ALU.mult))

        def xk_ap(g):
            return (xk_lo.ap()[:, g * 2048:(g + 1) * 2048] if g < 6
                    else xk_hi.ap()[:, (g - 6) * 2048:(g - 5) * 2048])

        for g in range(NCH):
            ev[("xr", g)] = mix(g, mur, murm,
                                lambda gg=g: xr.ap()[:, gg * 2048:(gg + 1) * 2048])
        for g in range(6):
            ev[("xk", g)] = mix(g, muk, mukm, lambda gg=g: xk_ap(gg))

        for q4 in range(NQ4):
            wi = load_w("wr", q4)
            for tt in range(NT):
                grp = (q4 * NT + tt) % 3
                w = [("swQ", wcum[wi]), ("dve", ev[("xr", NCH - 1)])]
                if ("rev", fi, q4 * NT + tt - 3) in ev:
                    w.append(("act", ev[("rev", fi, q4 * NT + tt - 3)]))
                for g in range(NCH):
                    ev[("rmm", fi, q4, tt)] = pe(
                        lambda e, t=tt, gg=g, gr=grp, ii=wi: e.matmul(
                            ps_mm.ap()[:, gr * 512:gr * 512 + 256],
                            xr.ap()[:, gg * 2048 + t * 128:gg * 2048 + (t + 1) * 128],
                            wslot(ii)[:, gg * 256:(gg + 1) * 256],
                            start=(gg == 0), stop=(gg == NCH - 1)),
                        waits=(w if g == 0 else ()))
                ev[("rev", fi, q4 * NT + tt)] = act(
                    lambda e, t=tt, q=q4, gr=grp: e.activation(
                        out=r_tm.ap()[:, t * 1024 + q * 256:t * 1024 + (q + 1) * 256],
                        in_=ps_mm.ap()[:, gr * 512:gr * 512 + 256],
                        func=AF.Tanh, scale=0.5),
                    waits=[("pe", ev[("rmm", fi, q4, tt)])])
            wfree[wi % 2] = P.n("pe")
        ev["r_done_pe"] = P.n("pe")
        for g in (6, 7):
            ev[("xk", g)] = mix(g, muk, mukm, lambda gg=g: xk_ap(gg),
                                wextra=[("pe", ev["r_done_pe"])])
        ev["rp1"] = dve(lambda e: e.tensor_scalar(
            out=r_tm.ap()[:, :], in0=r_tm.ap()[:, :], scalar1=1.0, scalar2=None,
            op0=ALU.add), waits=[("act", ev[("rev", fi, NQ4 * NT - 1)])])

        for q4 in range(NQ4):
            wi = load_w("wk", q4)
            for blk in range(2):
                dch = q4 * 2 + blk
                for qq in range(4):
                    grp = (dch * 4 + qq) % 3
                    w = [("swQ", wcum[wi]), ("dve", ev[("xk", 7)])]
                    if ("kev", fi, dch * 4 + qq - 3) in ev:
                        w.append(("dve", ev[("kev", fi, dch * 4 + qq - 3)]))
                    for g in range(NCH):
                        ev[("kmm", fi, dch, qq)] = pe(
                            lambda e, gg=g, b=blk, t=qq, gr=grp, ii=wi: e.matmul(
                                ps_mm.ap()[:, gr * 512:(gr + 1) * 512],
                                wslot(ii)[:, gg * 256 + b * 128:gg * 256 + (b + 1) * 128],
                                xk_ap(gg)[:, t * 512:(t + 1) * 512],
                                start=(gg == 0), stop=(gg == NCH - 1)),
                            waits=(w if g == 0 else ()))
                    ev[("krl", fi, dch, qq)] = act(
                        lambda e, gr=grp: e.activation(
                            out=kr_s.ap()[:, :],
                            in_=ps_mm.ap()[:, gr * 512:(gr + 1) * 512],
                            func=AF.Relu, scale=1.0),
                        waits=[("pe", ev[("kmm", fi, dch, qq)])])
                    ev[("kev", fi, dch * 4 + qq)] = dve(
                        lambda e, d=dch, t=qq: e.tensor_mul(
                            out=k_act.ap()[:, d * 2048 + t * 512:d * 2048 + (t + 1) * 512],
                            in0=kr_s.ap()[:, :], in1=kr_s.ap()[:, :]),
                        waits=[("act", ev[("krl", fi, dch, qq)])])
            wfree[wi % 2] = P.n("pe")

        for q4 in range(NQ4):
            wi = load_w("wv", q4)
            for tt in range(NT):
                grp = (q4 * NT + tt) % 3
                w = [("swQ", wcum[wi]), ("dve", ev[("kev", fi, 31)])]
                if ("vadd", fi, q4 * NT + tt - 3) in ev:
                    w.append(("dve", ev[("vadd", fi, q4 * NT + tt - 3)]))
                for g in range(NCH):
                    ev[("vmm", fi, q4, tt)] = pe(
                        lambda e, t=tt, gg=g, gr=grp, ii=wi: e.matmul(
                            ps_mm.ap()[:, gr * 512:gr * 512 + 256],
                            k_act.ap()[:, gg * 2048 + t * 128:gg * 2048 + (t + 1) * 128],
                            wslot(ii)[:, gg * 256:(gg + 1) * 256],
                            start=(gg == 0), stop=(gg == NCH - 1)),
                        waits=(w if g == 0 else ()))
                dve(lambda e, t=tt, q=q4, gr=grp: e.scalar_tensor_tensor(
                        out=prod_s.ap()[:, 0:256],
                        in0=ps_mm.ap()[:, gr * 512:gr * 512 + 256], scalar=0.25,
                        in1=r_tm.ap()[:, t * 1024 + q * 256:t * 1024 + (q + 1) * 256],
                        op0=ALU.mult, op1=ALU.mult),
                    waits=[("pe", ev[("vmm", fi, q4, tt)]), ("dve", ev["rp1"])])
                ev[("vadd", fi, q4 * NT + tt)] = dve(
                    lambda e, t=tt, q=q4: e.tensor_add(
                        out=res.ap()[:, t * 1024 + q * 256:t * 1024 + (q + 1) * 256],
                        in0=res.ap()[:, t * 1024 + q * 256:t * 1024 + (q + 1) * 256],
                        in1=prod_s.ap()[:, 0:256]))
                if q4 == NQ4 - 1:
                    ev[("res_ready", tt)] = dve(
                        lambda e: e.tensor_copy(out=dumpB.ap()[:, 64:128],
                                                in_=dumpB.ap()[:, 0:64]))
            wfree[wi % 2] = P.n("pe")

    def emit_attn(ps_S, ps_av, ps_pr):
        lnd = ev["ln_done"]
        ev["brow"] = gp_dma(lambda e: e.dma_start(out=bvrow[:], in_=bv_d),
                            waits=[("act", ev["stats"])])
        ev["brow"] = gp_dma(lambda e: e.dma_start(out=borow[:], in_=bo_d))
        for kt in range(KT):
            par = kt % 2
            w = []
            if ("strE", kt - 2) in ev:
                w.append(("act", ev[("strE", kt - 2)]))
            ev[("strD", kt)] = sp_dma(
                lambda e, k=kt, p=par: e.dma_start(out=strp[p].ap()[:, :], in_=strips_d[k]),
                waits=w)
            ev[("strE", kt)] = act(
                lambda e, k=kt, p=par: e.activation(
                    out=estrip.ap()[:, k * MASKW:(k + 1) * MASKW],
                    in_=strp[p].ap()[:, :], func=AF.Exp, scale=lamc[:, 0:1]),
                waits=[("spQ", ev[("strD", kt)]), ("spQ", ev["vecs"])])
        ev["strips"] = P.n("act")


        class ProjQ:
            def __init__(self, hp):
                self.hp = hp
                self.par = hp % 2
                self.j = 0
                self.wiq = load_w("wq", hp)
                self.wik = load_w("wka", hp)
                self.wiv = None
                dve(lambda e, p=self.par: e.memset(vbuf[p].ap()[:, :], 1.0),
                    waits=[("dve", ev["ln_done"]),
                           ("pe", ev.get(("pair_gone", hp - 2), 0))])
                ev[("vset", hp)] = P.n("dve")

            def pending(self):
                return self.j < 24

            def emit_one(self):
                j, hp, par = self.j, self.hp, self.par
                self.j += 1
                bank = j % 2
                sl = bank * 512
                war = []
                if ("prev", j - 2) in ev:
                    pass
                if j == 4:
                    wfree[self.wiq % 2] = P.n("pe")
                    self.wiv = load_w("wva", hp)
                if j < 8:
                    nm, wi, buf, bias = (("q", self.wiq, qbuf[par], bqp) if j < 4
                                         else ("k", self.wik, kbuf[par], bkp))
                    qq = j % 4
                    w = [("swQ", wcum[wi]), ("dve", ev["ln_done"]),
                         ("dve", ev.get("pr_nmul", 0)),
                         ("pe", ev.get(("pair_gone", hp - 2), 0))]
                    if ("prev", hp, j - 2) in ev:
                        w.append(("dve", ev[("prev", hp, j - 2)]))
                    for g in range(NCH):
                        ev[("prmm", hp, j)] = pe(
                            lambda e, gg=g, t=qq, ii=wi, s2=sl: e.matmul(
                                ps_pr.ap()[:, s2:s2 + 512],
                                wslot(ii)[:, gg * 128:(gg + 1) * 128],
                                ln_fm.ap()[:, gg * 2048 + t * 512:gg * 2048 + (t + 1) * 512],
                                start=(gg == 0), stop=(gg == NCH - 1)),
                            waits=(w if g == 0 else ()))
                    ev[("prev", hp, j)] = dve(
                        lambda e, b=buf, t=qq, bi=bias, hh=hp, s2=sl: e.tensor_scalar(
                            out=b.ap()[:, t * 512:(t + 1) * 512],
                            in0=ps_pr.ap()[:, s2:s2 + 512],
                            scalar1=bi[:, hh:hh + 1], scalar2=None, op0=ALU.add),
                        waits=[("pe", ev[("prmm", hp, j)])])
                    self.last_act = ev[("prev", hp, j)]
                    if j == 7:
                        ev[("qk_evac", hp)] = P.n("dve")
                else:
                    kt = j - 8
                    w = [("swQ", wcum[self.wiv]), ("swQ", ev["brow"]),
                         ("dve", ev[("qk_evac", hp)]),
                         ("dve", ev.get("pr_nmul", 0))]
                    if ("prve", hp, kt - 2) in ev:
                        w.append(("dve", ev[("prve", hp, kt - 2)]))
                    pe(lambda e, k=kt, hh=hp, s2=sl: e.matmul(
                            ps_pr.ap()[:, s2:s2 + 128],
                            onesb[0:1, 0:128],
                            bvrow.ap()[0:1, hh * 128:(hh + 1) * 128],
                            start=True, stop=False), waits=w)
                    for g in range(NCH):
                        ev[("prvm", hp, kt)] = pe(
                            lambda e, k=kt, gg=g, ii=self.wiv, s2=sl: e.matmul(
                                ps_pr.ap()[:, s2:s2 + 128],
                                ln_fm.ap()[:, gg * 2048 + k * 128:gg * 2048 + (k + 1) * 128],
                                wslot(ii)[:, gg * 128:(gg + 1) * 128],
                                start=False, stop=(gg == NCH - 1)))
                    ev[("prve", hp, kt)] = dve(
                        lambda e, k=kt, p=par: e.tensor_copy(
                            out=vbuf[p].ap()[:, k * 130:k * 130 + 130]
                                .rearrange("p (h c) -> p h c", c=65)[:, :, 0:64],
                            in_=ps_pr.ap()[:, (k % 2) * 512:(k % 2) * 512 + 128]
                                .rearrange("p (h c) -> p h c", c=64)),
                        waits=[("pe", ev[("prvm", hp, kt)]), ("dve", ev[("vset", hp)])])

            def finish(self):
                while self.pending():
                    self.emit_one()
                wfree[self.wik % 2] = P.n("pe")
                wfree[self.wiv % 2] = P.n("pe")
                ev[("vproj", self.hp)] = P.n("dve")
                ev[("pr_done_act", self.hp)] = P.n("act")
                ev[("pr_done_dve", self.hp)] = P.n("dve")
                self.last = (P.n("pe"), P.n("act"), P.n("dve"))

        projq = [None]

        def emit_unit(uidx, h, qh):
            hp, s = h // 2, h % 2
            par = hp % 2
            def avm_one(k2):
                pp = (k2 // 2) % 2
                w = [("act", ev[("exp2", uidx, k2 // 2)]),
                     ("dve", ev[("band", uidx, k2)])]
                if k2 == 0:
                    w.append(("dve", ev.get(("uncp", uidx - 1), 0)))
                for lq in range(2):
                    ev[("avmq", uidx, k2)] = pe(
                        lambda e, k=k2, q=lq, p2=pp: e.matmul(
                            ps_av.ap()[0:65, q * 512:(q + 1) * 512],
                            vbuf[par].ap()[:, k * 130 + s * 65:k * 130 + (s + 1) * 65],
                            attnT[p2].ap()[:, (k % 2) * 1024 + q * 512:
                                           (k % 2) * 1024 + q * 512 + 512],
                            start=(k == 0), stop=(k == KT - 1)),
                        waits=(w if lq == 0 else ()))

            def band_one(k2):
                pp = (k2 // 2) % 2
                w0b = min(max(k2 * 128 - MASKR, 0), L - MASKW)
                b0 = max(w0b, qh * 1024)
                b1 = min(w0b + MASKW, (qh + 1) * 1024)
                if b1 > b0:
                    o0 = (k2 % 2) * 1024 + (b0 - qh * 1024)
                    ev[("band", uidx, k2)] = dve(
                        lambda e, p2=pp, a2=o0, n2=b1 - b0, k=k2, ww=w0b, bb=b0:
                            e.tensor_mul(
                                out=attnT[p2].ap()[:, a2:a2 + n2],
                                in0=attnT[p2].ap()[:, a2:a2 + n2],
                                in1=estrip.ap()[:, k * MASKW + (bb - ww):
                                                k * MASKW + (bb - ww) + n2]),
                        waits=[("act", ev[("exp2", uidx, k2 // 2)])])
                else:
                    ev[("band", uidx, k2)] = 0

            for tp2 in range(KT // 2 + 1):
                if tp2 < KT // 2:
                    w = [("dve", ev[("vproj", hp)])]
                    if ("exp2", uidx, tp2 - 1) in ev:
                        w.append(("act", ev[("exp2", uidx, tp2 - 1)]))
                    elif ("exp2", uidx - 1, 7) in ev:
                        w.append(("act", ev[("exp2", uidx - 1, 7)]))
                    first = True
                    for kk in (2 * tp2, 2 * tp2 + 1):
                        for lq in range(2):
                            ev[("smm", uidx, tp2)] = pe(
                                lambda e, k=kk, q=lq, ss=s: e.matmul(
                                    ps_S.ap()[:, (k % 2) * 1024 + q * 512:
                                              (k % 2) * 1024 + q * 512 + 512],
                                    kbuf[par].ap()[ss * 64:(ss + 1) * 64,
                                                   k * 128:(k + 1) * 128],
                                    qbuf[par].ap()[ss * 64:(ss + 1) * 64,
                                                   qh * 1024 + q * 512:
                                                   qh * 1024 + (q + 1) * 512],
                                    start=True, stop=True),
                                waits=(w if first else ()))
                            first = False
                    w = [("pe", ev[("smm", uidx, tp2)]), ("dve", ev["consts"]),
                         ("act", ev["strips"])]
                    if ("avmq", uidx, 2 * tp2 - 3) in ev:
                        w.append(("pe", ev[("avmq", uidx, 2 * tp2 - 3)]))
                    elif ("avmq", uidx - 1, KT - 1) in ev:
                        w.append(("pe", ev[("avmq", uidx - 1, KT - 1)]))
                    ev[("exp2", uidx, tp2)] = act(
                        lambda e, p2=tp2 % 2: e.activation(
                            out=attnT[p2].ap()[:, 0:2048],
                            in_=ps_S.ap()[:, 0:2048],
                            func=AF.Exp, bias=m4c[:, 0:1], scale=0.125),
                        waits=w)
                if tp2 >= 1:
                    for k2 in (2 * tp2 - 2, 2 * tp2 - 1):
                        band_one(k2)
                        avm_one(k2)
                if tp2 == 1 and pending[0] is not None:
                    flush_norm_rows()
                if tp2 == 3 and pending[0] is not None:
                    flush_norm_mul()
                if projq[0] is not None and projq[0].pending():
                    projq[0].emit_one()
                    if projq[0].pending() and tp2 % 2 == 0:
                        projq[0].emit_one()
            ev[("uncp", uidx)] = dve(
                lambda e: e.tensor_copy(out=avsb.ap()[0:65, 0:1024],
                                        in_=ps_av.ap()[0:65, 0:1024]),
                waits=[("pe", ev[("avmq", uidx, KT - 1)]),
                       ("dve", ev.get("pr_nmul", 0))])
            pending[0] = (uidx, hp, s, qh)

        pending = [None]

        def flush_norm_rows():
            uidx, hp, s, qh = pending[0]
            act(lambda e: e.activation(out=avsb.ap()[64:65, 0:1024],
                                       in_=avsb.ap()[64:65, 0:1024],
                                       func=AF.Ln, scale=1.0),
                waits=[("dve", ev[("uncp", uidx)]),
                       ("pe", ev.get(("unbc", uidx - 1), 0))])
            ev[("nrc", uidx)] = act(lambda e: e.activation(
                out=recip_b.ap()[0:1, 0:1024], in_=avsb.ap()[64:65, 0:1024],
                func=AF.Exp, scale=-1.0))

        def flush_norm_mul():
            uidx, hp, s, qh = pending[0]
            pending[0] = None
            w = [("act", ev[("nrc", uidx)]),
                 ("act", P.n("act")), ("dve", P.n("dve"))]
            for q in range(2):
                ev[("unbc", uidx)] = pe(
                    lambda e, qq=q: e.matmul(
                        ps_pr.ap()[0:64, qq * 512:(qq + 1) * 512],
                        onesb[0:1, 0:64],
                        recip_b.ap()[0:1, qq * 512:(qq + 1) * 512],
                        start=True, stop=True),
                    waits=(w if q == 0 else ()))
            ev[("unmul", uidx)] = dve(
                lambda e, ss=s, hh=hp, qq=qh: e.tensor_mul(
                    out=aout.ap()[ss * 64:(ss + 1) * 64,
                                  hh * 2048 + qq * 1024:hh * 2048 + (qq + 1) * 1024],
                    in0=avsb.ap()[0:64, 0:1024],
                    in1=ps_pr.ap()[0:64, 0:1024]),
                waits=[("pe", ev[("unbc", uidx)])])
            ev["pr_nmul"] = ev[("unmul", uidx)]

        projq[0] = ProjQ(0)
        projq[0].finish()
        uidx = 0
        for hp in range(8):
            for (h, qh) in [(2 * hp, 0), (2 * hp, 1), (2 * hp + 1, 0), (2 * hp + 1, 1)]:
                if (h, qh) == (2 * hp, 1) and hp < 7:
                    projq[0] = ProjQ(hp + 1)
                emit_unit(uidx, h, qh)
                uidx += 1
            if projq[0] is not None and projq[0].hp == hp + 1:
                projq[0].finish()
            ev[("pair_gone", hp)] = P.n("pe")
        flush_norm_rows()
        flush_norm_mul()
        ev[("pair_gone", 7)] = P.n("pe")
        ev["S_free_act"] = P.n("act")
        ev["attn_done"] = P.n("dve")

        for q4 in range(NQ4):
            wi = load_w("wo", q4)
            for tt in range(NT):
                grp = (q4 * NT + tt) % 4
                w = [("swQ", wcum[wi]), ("dve", ev["attn_done"]),
                     ("swQ", ev["brow"]), ("act", ev["S_free_act"])]
                if ("oadd", q4 * NT + tt - 4) in ev:
                    w.append(("dve", ev[("oadd", q4 * NT + tt - 4)]))
                pe(lambda e, q=q4, gr=grp: e.matmul(
                        ps_S.ap()[:, gr * 512:gr * 512 + 256],
                        onesb[0:1, 0:128],
                        borow.ap()[0:1, q * 256:(q + 1) * 256],
                        start=True, stop=False), waits=w)
                for g in range(NCH):
                    ev[("omm", q4, tt)] = pe(
                        lambda e, t=tt, gg=g, gr=grp, ii=wi: e.matmul(
                            ps_S.ap()[:, gr * 512:gr * 512 + 256],
                            aout.ap()[:, gg * 2048 + t * 128:gg * 2048 + (t + 1) * 128],
                            wslot(ii)[:, gg * 256:(gg + 1) * 256],
                            start=False, stop=(gg == NCH - 1)))
                ev[("oadd", q4 * NT + tt)] = dve(
                    lambda e, t=tt, q=q4, gr=grp: e.tensor_add(
                        out=res.ap()[:, t * 1024 + q * 256:t * 1024 + (q + 1) * 256],
                        in0=res.ap()[:, t * 1024 + q * 256:t * 1024 + (q + 1) * 256],
                        in1=ps_S.ap()[:, gr * 512:gr * 512 + 256]),
                    waits=[("pe", ev[("omm", q4, tt)])])
                if q4 == NQ4 - 1:
                    ev[("res_ready", tt)] = dve(
                        lambda e: e.tensor_copy(out=dumpB.ap()[:, 64:128],
                                                in_=dumpB.ap()[:, 0:64]))
            wfree[wi % 2] = P.n("pe")
        ev["o_done"] = P.n("dve")

    done = [0]

    def grp13(tag, emits):
        with ExitStack() as psx:
            ps_mm = psx.enter_context(nc.psum_tensor(f"ps_mm{tag}", [128, 1536], F32))
            ps_tr = [psx.enter_context(nc.psum_tensor(f"ps_tr{tag}{p}", [128, 1024], BF16))
                     for p in (0, 1)]
            for e in emits:
                if done[0] >= n_phases:
                    return
                if e == "ln":
                    emit_ln(ps_tr)
                elif e == "lnx":
                    emit_ln(ps_tr, tr_extra=ev["o_done"])
                else:
                    emit_ffn(ps_mm)
                done[0] += 1

    grp13("1", ["ln", "ffn", "ln"])
    if done[0] < n_phases:
        with ExitStack() as psx:
            ps_S = psx.enter_context(nc.psum_tensor("ps_S", [128, 2048], F32))
            ps_av = psx.enter_context(nc.psum_tensor("ps_av", [128, 1024], F32))
            ps_pr = psx.enter_context(nc.psum_tensor("ps_pr", [128, 1024], F32))
            emit_attn(ps_S, ps_av, ps_pr)
            done[0] += 1
    grp13("2", ["lnx", "ffn"])

    out_r = out_d.rearrange("(t p) d -> p t d", p=128)
    if n_phases >= 6:
        stw = [("dve", ev[("vadd", 1, NQ4 * NT - 1)])]
    else:
        pe(lambda e: e.nop())
        act(lambda e: e.activation(out=dumpB.ap()[:, 0:512], in_=dumpB.ap()[:, 0:512],
                                   func=AF.Identity, scale=1.0))
        dve(lambda e: e.tensor_copy(out=dumpB.ap()[:, 0:512], in_=dumpB.ap()[:, 512:1024]))
        stw = [("pe", P.n("pe")), ("act", P.n("act")), ("dve", P.n("dve"))]
    st = sp_dma(lambda e: e.dma_start(out=out_r, in_=res_r), waits=stw)
    if dbg:
        dbg_list = [("dbg_lnfm", ln_fm, BF16), ("dbg_aout", aout, BF16),
                    ("dbg_rtm", r_tm, BF16), ("dbg_kact", k_act, BF16),
                    ("dbg_avsb", avsb, F32), ("dbg_xr", xr, BF16),
                    ("dbg_q0", qbuf[0], BF16), ("dbg_k0", kbuf[0], BF16),
                    ("dbg_v0", vbuf[0], BF16)]
        for nm, t, dt_ in dbg_list:
            d = nc.dram_tensor(nm, [128, int(np.prod(t.shape[1:]))], dt_,
                               kind="ExternalOutput").ap()
            st = sp_dma(lambda e, dd=d, rr=t: e.dma_start(out=dd, in_=rr.ap()[:, :]),
                        waits=stw)
    P.op("sp", lambda e: e.nop(), waits=[("spQ", P.n("spQ"))])

    with ExitStack() as bctx:
        sems = {s: bctx.enter_context(nc.semaphore(s)) for s in SEMS}
        for s in SEMS:
            nc.gpsimd.sem_clear(sems[s])
        nc.all_engine_barrier()
        block = bctx.enter_context(nc.Block())

        @block.tensor
        def _(e):
            P.replay("pe", e, sems)

        @block.scalar
        def _(e):
            P.replay("act", e, sems)

        @block.vector
        def _(e):
            P.replay("dve", e, sems)

        @block.sync
        def _(e):
            P.replay("sp", e, sems)

        @block.gpsimd
        def _(e):
            P.replay("gp", e, sems)

    return nc


def _prep_w_quarters(w):
    wt = np.ascontiguousarray(np.asarray(w, np.float32).T)
    t = wt.reshape(NCH, 128, NQ4, 256)
    t = np.ascontiguousarray(t.transpose(2, 1, 0, 3))
    return t.reshape(NQ4, 128, NCH * 256)


def _prep_w_pairs(w):
    wt = np.ascontiguousarray(np.asarray(w, np.float32).T)
    t = wt.reshape(NCH, 128, 8, 128)
    t = np.ascontiguousarray(t.transpose(2, 1, 0, 3))
    return t.reshape(8, 128, NCH * 128)


def _prep_col(v):
    return np.ascontiguousarray(np.asarray(v, np.float32).reshape(8, 128).T)


def _make_strips():
    out = np.zeros((KT, 128, MASKW), dtype=np.float64)
    p = np.arange(128)
    j = np.arange(MASKW)
    for kt in range(KT):
        w0 = min(max(kt * 128 - MASKR, 0), L - MASKW)
        k = kt * 128 + p
        delta = np.abs(k[:, None] - (w0 + j)[None, :]).astype(np.float64)
        out[kt] = 0.8 * np.exp(-((delta - 1.0) ** 2) / 0.5) + \
            0.4 * np.exp(-((delta - 30.0) ** 2) / 8.0)
    return out.astype(np.float32)


_NC_CACHE = {}


def kernel(L0_attn_x, L0_attn_Wq, L0_attn_bq, L0_attn_Wk, L0_attn_bk,
           L0_attn_Wv, L0_attn_bv, L0_attn_Wo, L0_attn_bo,
           L0_attn_prior_lambda, L0_ffn_W_r, L0_ffn_W_k, L0_ffn_W_v,
           L0_ffn_mu_r, L0_ffn_mu_k):
    x = np.ascontiguousarray(np.asarray(L0_attn_x, dtype=np.float32))
    B = x.shape[0]
    assert x.shape == (B, L, D)

    common = dict(
        wr=_prep_w_quarters(L0_ffn_W_r),
        wk=_prep_w_quarters(L0_ffn_W_k),
        wv=_prep_w_quarters(L0_ffn_W_v),
        wo=_prep_w_quarters(L0_attn_Wo),
        wq=_prep_w_pairs(L0_attn_Wq),
        wka=_prep_w_pairs(L0_attn_Wk),
        wva=_prep_w_pairs(L0_attn_Wv),
        bq=_prep_col(L0_attn_bq),
        bk=_prep_col(L0_attn_bk),
        bv=np.ascontiguousarray(np.asarray(L0_attn_bv, np.float32).reshape(1, D)),
        bo=np.ascontiguousarray(np.asarray(L0_attn_bo, np.float32).reshape(1, D)),
        mur=_prep_col(L0_ffn_mu_r),
        muk=_prep_col(L0_ffn_mu_k),
        lam=np.ascontiguousarray(np.broadcast_to(np.asarray(L0_attn_prior_lambda, np.float32).reshape(1, 1), (128, 1))),
        strips=_make_strips(),
        ident=np.eye(128, dtype=np.float32),
        onesf=np.ones((128, 128), dtype=np.float32),
    )

    if "nc" not in _NC_CACHE:
        _NC_CACHE["nc"] = build_nc()
    nc = _NC_CACHE["nc"]

    in_maps = [dict(common, x=np.ascontiguousarray(x[b])) for b in range(B)]
    res = run_bass_kernel_spmd(nc, in_maps, core_ids=list(range(B)))
    return np.stack([r["out"] for r in res.results], axis=0)


if __name__ == "__main__":
    nc = build_nc()
    print("built:", nc.n_instructions(), "instructions")
